# version 01
# baseline (speedup 1.0000x reference)
import numpy as np
import ml_dtypes
from contextlib import ExitStack

import concourse.bass as bass
import concourse.mybir as mybir
from concourse.bass_utils import run_bass_kernel_spmd
from concourse.library_config import attnmlp
from concourse.bass_isa import ReduceOp

F32 = mybir.dt.float32
BF16 = mybir.dt.bfloat16
U32 = mybir.dt.uint32
AF = mybir.ActivationFunctionType
ALU = mybir.AluOpType
AX = mybir.AxisListType.X

C = 1024
T = 4096
NT = 16
TPC = NT * 128
NER = 64
NE = 66
I_E = 704
NI = 6
ILEN = [128, 128, 128, 128, 128, 64]
IOFF = [0, 128, 256, 384, 512, 640]
L = 256
NHP = 8
SCALE = 0.125


class Sems:
    def __init__(self, nc, cx):
        self.nc, self.cx, self.h, self.n = nc, cx, {}, {}

    def get(self, name):
        if name not in self.h:
            self.h[name] = self.cx.enter_context(self.nc.semaphore(name))
            self.n[name] = 0
        return self.h[name]

    def bump(self, name, k=1):
        h = self.get(name)
        self.n[name] += k
        return h, self.n[name]

    def val(self, name):
        self.get(name)
        return self.n[name]


class Em:
    ENGS = ("tensor", "vector", "scalar", "gpsimd", "sync")

    def __init__(self):
        self.streams = {e: [] for e in self.ENGS}

    def on(self, eng, fn):
        self.streams[eng].append(fn)

    def emit(self, block):
        for eng in self.ENGS:
            fns = self.streams[eng]
            if not fns:
                continue

            def run(engine, fns=fns):
                for f in fns:
                    f(engine)
            getattr(block, eng)(run)
        self.streams = {e: [] for e in self.ENGS}


def build_program():
    nc = bass.Bass("TRN2", target_bir_lowering=False)

    def inp(name, shape, dt=F32):
        return nc.dram_tensor(name, list(shape), dt, kind="ExternalInput").ap()

    x_in = inp("x", [TPC, C])
    wq_in = inp("wq", [C, C]); bq_in = inp("bq", [C])
    wkd_in = inp("wkd", [C, 2 * L]); bkd_in = inp("bkd", [2 * L])
    wku_in = inp("wku", [L, 2 * C]); bku_in = inp("bku", [2 * C])
    wo_in = inp("wo", [C, C]); bo_in = inp("bo", [C])
    gw_in = inp("gw", [NER, C]); gb_in = inp("gb", [NER])
    wg_in = inp("wg", [NER, I_E, C])
    wu_in = inp("wu", [NER, I_E, C])
    wd_in = inp("wd", [NER, C, I_E])
    sg_in = inp("sg", [2 * I_E, C])
    su_in = inp("su", [2 * I_E, C])
    sd_in = inp("sd", [C, 2 * I_E])
    ident_in = inp("ident", [128, 128], BF16)
    identf_in = inp("identf", [128, 128])
    iota64_in = inp("iota64", [128, NER])
    iotag_in = inp("iotag", [128, NER])
    onesf_in = inp("onesf", [1, 512])
    binm_in = inp("binm", [2, 128, 128], BF16)

    y_out = nc.dram_tensor("y", [TPC, C], F32, kind="ExternalOutput").ap()

    wg_bf = nc.dram_tensor("wg_bf", [NE, I_E, C], BF16, kind="Internal").ap()
    wu_bf = nc.dram_tensor("wu_bf", [NE, I_E, C], BF16, kind="Internal").ap()
    wd_bf = nc.dram_tensor("wd_bf", [NE, C, 768], BF16, kind="Internal").ap()
    gw_bf = nc.dram_tensor("gw_bf", [NER, C], BF16, kind="Internal").ap()
    lat_own = nc.dram_tensor("lat_own", [TPC, L], BF16, kind="Internal").ap()
    lat_all = nc.dram_tensor("lat_all", [T, L], BF16, kind="Internal").ap()

    cx = ExitStack()
    S = Sems(nc, cx)
    EM = Em()

    def sb(stack, name, shape, dt):
        return stack.enter_context(nc.sbuf_tensor(name, list(shape), dt))

    def psum(stack, name, shape, dt):
        return stack.enter_context(nc.psum_tensor(name, list(shape), dt))

    x_tm = sb(cx, "x_tm", [128, NT, C], F32)
    ident = sb(cx, "ident_sb", [128, 128], BF16)
    identf = sb(cx, "identf_sb", [128, 128], F32)
    iota64 = sb(cx, "iota64_sb", [128, NER], F32)
    iotag = sb(cx, "iotag_sb", [128, NER], F32)
    onesf = sb(cx, "onesf_sb", [1, 512], F32)
    gbb = sb(cx, "gbb", [128, NER], F32)
    gb_row = sb(cx, "gb_row", [1, NER], F32)
    lnst = sb(cx, "lnst", [128, 12], F32)
    lnmv = sb(cx, "lnmv", [128, 2], F32)
    lninv = sb(cx, "lninv", [128, 1], F32)
    lnnmi = sb(cx, "lnnmi", [128, 1], F32)
    xlnstage = sb(cx, "xlnstage", [128, C], BF16)
    zpad = sb(cx, "zpad", [128, 512], BF16)

    def emit_phase0():
        hld = S.get("ld")
        loads = [(x_tm[:, :, :], x_in.rearrange("(a p) n -> p a n", p=128)),
                 (ident[:, :], ident_in[:, :]),
                 (identf[:, :], identf_in[:, :]),
                 (iota64[:, :], iota64_in[:, :]),
                 (iotag[:, :], iotag_in[:, :]),
                 (onesf[:, :], onesf_in[:, :]),
                 (gb_row[:, :], gb_in[None, :])]
        for _ in loads:
            S.bump("ld", 16)

        ld_total = S.val("ld")

        def f_sy(sy, loads=loads, hld=hld, ld_total=ld_total):
            for dst, src in loads:
                sy.dma_start(dst, src).then_inc(hld, 16)
            sy.wait_ge(hld, ld_total)
        EM.on("sync", f_sy)

        hcast = S.get("cast")
        jobs = [(gw_bf[:, :], gw_in[:, :])]
        for e in range(NE):
            if e < NER:
                srcs = (wg_in[e], wu_in[e], wd_in[e])
            else:
                hf = e - NER
                srcs = (sg_in[hf * I_E:(hf + 1) * I_E, :],
                        su_in[hf * I_E:(hf + 1) * I_E, :],
                        sd_in[:, hf * I_E:(hf + 1) * I_E])
            for dst, src in zip((wg_bf[e], wu_bf[e], wd_bf[e, :, 0:I_E]), srcs):
                jobs.append((dst, src))
            jobs.append((wd_bf[e, :, I_E:768], "ZPAD"))
        for _ in jobs:
            S.bump("cast", 16)

        hzp = S.get("zp_done")

        def f_zp(v):
            v.memset(zpad[:, :], 0.0).then_inc(hzp, 1)
        EM.on("vector", f_zp)
        S.bump("zp_done")

        def f_gp(g, jobs=jobs, hcast=hcast):
            g.load_library(attnmlp)
            g.wait_ge(hzp, 1)
            total = 0
            for n_, (dst, src) in enumerate(jobs):
                if isinstance(src, str):
                    g.dma_start(out=dst, in_=zpad[:, :]).then_inc(hcast, 16)
                else:
                    g.dma_start(out=dst, in_=src).then_inc(hcast, 16)
                total += 16
                if n_ == 0 or (n_ - 1) % 4 == 3:
                    g.wait_ge(hcast, total)
        EM.on("gpsimd", f_gp)
        return S.val("ld")

    def emit_gbb(proute, ld_all):
        hmg = S.get("mm_gbb")
        S.bump("mm_gbb")

        def f_t(t_, ld_all=ld_all):
            t_.wait_ge(S.h["ld"], ld_all)
            t_.matmul(proute[:, 0:NER], onesf[0:1, 0:128], gb_row[0:1, :],
                      start=True, stop=True).then_inc(hmg, 1)
        EM.on("tensor", f_t)

        hgd = S.get("gbb_done")
        S.bump("gbb_done")

        def f_v(v):
            v.wait_ge(hmg, 1)
            v.tensor_copy(gbb[:, :], proute[:, 0:NER]).then_inc(hgd, 1)
        EM.on("vector", f_v)

    def emit_ln_route(cc, xln, gateT, rt, rw, rwT, proute, ptrans,
                      x_ready, bar_wait):
        hv1, hs1, hv2, hap = S.get("ln_v1"), S.get("ln_s1"), S.get("ln_v2"), S.get("ln_ap")
        hlnt, hlev = S.get("lnt"), S.get("lnt_ev")
        hlgt, hsig, hrtd = S.get("lgt"), S.get("sig"), S.get("rt_done")
        hrwt, hrwe = S.get("rwt"), S.get("rwt_ev")
        hld, hwld, hcast, hgd = S.get("ld"), S.get("wld"), S.get("cast"), S.get("gbb_done")

        cast_gw = 16 if cc == 0 else S.val("cast")

        gate_tgt_ = S.val("wld") + 128

        def f_gt(sy, cc=cc, bar_wait=bar_wait, gateT=gateT, gate_tgt_=gate_tgt_):
            if bar_wait is not None:
                bar_wait(sy)
            sy.wait_ge(hcast, cast_gw)
            for cch in range(8):
                sy.dma_start(gateT[:, cch, :], gw_bf[:, cch * 128:(cch + 1) * 128],
                             transpose=True).then_inc(hwld, 16)
            sy.wait_ge(hwld, gate_tgt_)
        EM.on("sync", f_gt)
        for _ in range(8):
            S.bump("wld", 16)
        gate_tgt = S.val("wld")

        for t in range(NT):
            S.bump("ln_v1")

            def f_v1(v, t=t, first=(t == 0), x_ready=x_ready, bar_wait=bar_wait):
                if first:
                    if bar_wait is not None:
                        bar_wait(v)
                    if x_ready is not None:
                        v.wait_ge(x_ready[0], x_ready[1])
                v.bn_stats(lnst[:, 0:6], x_tm[:, t, 0:512])
                v.bn_stats(lnst[:, 6:12], x_tm[:, t, 512:1024])
                v.drain()
                v.bn_aggr(lnmv[:, :], lnst[:, :])
                v.drain()
                v.tensor_scalar(lninv[:, :], lnmv[:, 1:2], 1e-5, None, ALU.add).then_inc(hv1, 1)
            EM.on("vector", f_v1)

            w1 = S.val("ln_v1"); S.bump("ln_s1")

            def f_s1(s, w1=w1, first=(t == 0), bar_wait=bar_wait):
                if first and bar_wait is not None:
                    bar_wait(s)
                s.wait_ge(hv1, w1)
                s.activation(lninv[:, :], lninv[:, :], AF.Sqrt).then_inc(hs1, 1)
            EM.on("scalar", f_s1)

            w2 = S.val("ln_s1"); S.bump("ln_v2")

            def f_v2(v, w2=w2):
                v.wait_ge(hs1, w2)
                v.reciprocal(lninv[:, :], lninv[:, :])
                v.drain()
                v.tensor_scalar(lnnmi[:, :], lnmv[:, 0:1], lninv[:, 0:1], -1.0,
                                ALU.mult, ALU.mult).then_inc(hv2, 1)
            EM.on("vector", f_v2)

            w3 = S.val("ln_v2")
            wlnt_prev = S.val("lnt")
            S.bump("ln_ap")

            def f_s2(s, t=t, w3=w3, wlnt_prev=wlnt_prev):
                s.wait_ge(hv2, w3)
                s.wait_ge(hlnt, wlnt_prev)
                s.activation(xlnstage[:, :], x_tm[:, t, :], AF.Identity,
                             bias=lnnmi[:, 0:1], scale=lninv[:, 0:1]).then_inc(hap, 1)
            EM.on("scalar", f_s2)

            wap = S.val("ln_ap")
            wlev_prev = S.val("lnt_ev")
            S.bump("lnt")

            def f_t1(t_, wap=wap, wlev=wlev_prev, first=(t == 0), bar_wait=bar_wait):
                if first and bar_wait is not None:
                    bar_wait(t_)
                t_.wait_ge(hap, wap)
                t_.wait_ge(hlev, wlev)
                for cch in range(8):
                    ins = t_.transpose(ptrans[:, cch * 128:(cch + 1) * 128],
                                       xlnstage[:, cch * 128:(cch + 1) * 128], ident[:, :])
                ins.then_inc(hlnt, 1)
            EM.on("tensor", f_t1)

            wlnt = S.val("lnt")
            S.bump("lnt_ev")

            def f_v3(v, t=t, wlnt=wlnt, xln=xln):
                v.wait_ge(hlnt, wlnt)
                v.tensor_copy(xln[:, :, t * 128:(t + 1) * 128],
                              ptrans.rearrange("p (a n) -> p a n", a=8)).then_inc(hlev, 1)
            EM.on("vector", f_v3)

            wlev2 = S.val("lnt_ev")
            wsig_prev = S.val("sig")
            wrwe_prev = S.val("rwt_ev")
            S.bump("lgt")

            def f_t2(t_, t=t, wlev2=wlev2, wsig=wsig_prev, wrwe=wrwe_prev,
                     first=(t == 0), gate_tgt=gate_tgt, xln=xln, gateT=gateT):
                t_.wait_ge(hlev, wlev2)
                t_.wait_ge(hsig, wsig)
                t_.wait_ge(hrwe, wrwe)
                if first:
                    t_.wait_ge(hwld, gate_tgt)
                    t_.wait_ge(hgd, 1)
                for cch in range(8):
                    ins = t_.matmul(proute[:, 0:NER], xln[:, cch, t * 128:(t + 1) * 128],
                                    gateT[:, cch, :], start=(cch == 0), stop=(cch == 7))
                ins.then_inc(hlgt, 1)
            EM.on("tensor", f_t2)

            wlgt = S.val("lgt")
            wrtd_prev = S.val("rt_done")
            S.bump("sig")

            def f_s3(s, wlgt=wlgt, wrtd=wrtd_prev, rt=rt):
                s.wait_ge(hlgt, wlgt)
                s.wait_ge(hrtd, wrtd)
                s.activation(rt["s"][:, :], proute[:, 0:NER], AF.Sigmoid).then_inc(hsig, 1)
            EM.on("scalar", f_s3)

            wsig = S.val("sig")
            S.bump("rt_done")

            def f_v4(v, t=t, wsig=wsig, first=(t == 0 and cc == 0), rt=rt, rw=rw):
                v.wait_ge(hsig, wsig)
                if first:
                    v.wait_ge(hgd, 1)
                v.tensor_tensor(rt["sfc"], rt["s"][:, :], gbb[:, :], ALU.add)
                v.drain()
                for g in range(8):
                    v.max(rt["top8"], rt["sfc"][:, g * 8:(g + 1) * 8])
                    v.drain()
                    v.tensor_tensor(rt["gs"][:, g:g + 1], rt["top8"][:, 0:1],
                                    rt["top8"][:, 1:2], ALU.add)
                    v.drain()
                v.max(rt["top8"], rt["gs"])
                v.drain()
                v.max_index(rt["gidx"], rt["top8"], rt["gs"])
                v.drain()
                v.tensor_copy(rt["gidxf"], rt["gidx"])
                v.drain()
                v.memset(rt["emask"], 0.0)
                for k in range(4):
                    v.drain()
                    v.tensor_scalar(rt["tmp"], iotag[:, :], rt["gidxf"][:, k:k + 1],
                                    None, ALU.is_equal)
                    v.drain()
                    v.tensor_tensor(rt["emask"], rt["emask"], rt["tmp"], ALU.add)
                v.drain()
                v.tensor_tensor(rt["masked"], rt["sfc"], rt["emask"], ALU.mult)
                v.drain()
                v.max(rt["v8"], rt["masked"])
                v.drain()
                v.max_index(rt["i8"], rt["v8"], rt["masked"])
                v.drain()
                v.tensor_copy(rt["i8f"], rt["i8"])
                v.drain()
                v.memset(rt["sel"], 0.0)
                for k in range(8):
                    v.drain()
                    v.tensor_scalar(rt["tmp"], iota64[:, :], rt["i8f"][:, k:k + 1],
                                    None, ALU.is_equal)
                    v.drain()
                    v.tensor_tensor(rt["sel"], rt["sel"], rt["tmp"], ALU.add)
                v.drain()
                v.tensor_tensor(rt["sw"], rt["s"][:, :], rt["sel"], ALU.mult)
                v.drain()
                v.tensor_reduce(rt["rs"], rt["sw"], AX, ALU.add)
                v.drain()
                v.tensor_scalar(rt["rs"], rt["rs"], 1e-20, None, ALU.add)
                v.drain()
                v.reciprocal(rt["inv"], rt["rs"])
                v.drain()
                v.tensor_scalar(rw[:, t, :], rt["sw"], rt["inv"][:, 0:1], 1.25,
                                ALU.mult, ALU.mult).then_inc(hrtd, 1)
            EM.on("vector", f_v4)

            wrtd = S.val("rt_done")
            wrwe2 = S.val("rwt_ev")
            S.bump("rwt")

            def f_t3(t_, t=t, wrtd=wrtd, wrwe=wrwe2, rw=rw):
                t_.wait_ge(hrtd, wrtd)
                t_.wait_ge(hrwe, wrwe)
                t_.transpose(proute[0:64, 128:256], rw[:, t, :], identf[:, :]).then_inc(hrwt, 1)
            EM.on("tensor", f_t3)

            wrwt = S.val("rwt")
            S.bump("rwt_ev")

            def f_v5(v, t=t, wrwt=wrwt, rwT=rwT):
                v.wait_ge(hrwt, wrwt)
                v.tensor_copy(rwT[0:NER, t, :], proute[0:64, 128:256]).then_inc(hrwe, 1)
            EM.on("vector", f_v5)

    def emit_ffn(cc, xln, rw, rwT, wgT, wuT, wdT, hbuf, silu_g, rwb, rwrow,
                 pg, pu, pd):
        hwld, hcast = S.get("wld"), S.get("cast")
        hsg, hsu, hsd = S.get("s_g"), S.get("s_u"), S.get("s_d")
        hsilu, hsh, hsx = S.get("s_silu"), S.get("s_h"), S.get("s_x")
        hrwb, hrwrow, hrwe = S.get("rwb"), S.get("rwrow"), S.get("rwt_ev")

        rwt_all = S.val("rwt_ev")
        for e in range(NE):
            par = e % 2
            routed = e < NER
            eidx = cc * NE + e

            cast_tgt = 16 + 64 * (e + 1)
            wfree_tgt = (eidx - 1) * NT if eidx >= 2 else 0

            wld_tgt_ = S.val("wld") + 352

            def f_wl(sy, e=e, par=par, cast_tgt=cast_tgt, wfree_tgt=wfree_tgt,
                     wgT=wgT, wuT=wuT, wdT=wdT, wld_tgt_=wld_tgt_):
                sy.wait_ge(hcast, cast_tgt)
                if wfree_tgt > 0:
                    sy.wait_ge(hsd, wfree_tgt)
                for cch in range(8):
                    sy.dma_start(wgT[:, par, cch, :], wg_bf[e, :, cch * 128:(cch + 1) * 128],
                                 transpose=True).then_inc(hwld, 16)
                for cch in range(8):
                    sy.dma_start(wuT[:, par, cch, :], wu_bf[e, :, cch * 128:(cch + 1) * 128],
                                 transpose=True).then_inc(hwld, 16)
                for ic in range(NI):
                    sy.dma_start(wdT[0:128, par, ic, :],
                                 wd_bf[e, :, IOFF[ic]:IOFF[ic] + 128],
                                 transpose=True).then_inc(hwld, 16)
                sy.wait_ge(hwld, wld_tgt_)
            EM.on("sync", f_wl)
            for _ in range(22):
                S.bump("wld", 16)
            wld_tgt = S.val("wld")

            if routed:
                rwbfree = (eidx - 1) * 4 * NI if eidx >= 2 else 0
                S.bump("rwrow", 16)
                rwrow_tgt = S.val("rwrow")
                S.bump("rwb")

                def f_rb(g, e=e, par=par, rwt_all=rwt_all, rwbfree=rwbfree,
                         rwrow_tgt=rwrow_tgt, rwT=rwT, rwrow=rwrow, rwb=rwb):
                    g.wait_ge(hrwe, rwt_all)
                    if rwbfree > 0:
                        g.wait_ge(hsh, rwbfree)
                    g.dma_start(rwrow[0:1, par, :], rwT[e:e + 1, :, :]).then_inc(hrwrow, 16)
                    g.wait_ge(hrwrow, rwrow_tgt)
                    g.partition_broadcast(rwb[:, par, :], rwrow[0:1, par, :],
                                          channels=128).then_inc(hrwb, 1)
                EM.on("gpsimd", f_rb)
            rwb_tgt = S.val("rwb")

            for tg in range(4):
                for ic in range(NI):
                    il = ILEN[ic]
                    uq = S.val("s_u") + 1
                    S.bump("s_g")
                    S.bump("s_u")

                    def f_gu(t_, tg=tg, ic=ic, il=il, par=par, uq=uq,
                             wld_tgt=wld_tgt, rwt_all=rwt_all,
                             first=(tg == 0 and ic == 0), e0=(e == 0),
                             xln=xln, wgT=wgT, wuT=wuT):
                        if first:
                            t_.wait_ge(hwld, wld_tgt)
                            if e0:
                                t_.wait_ge(hrwe, rwt_all)
                        if uq >= 2:
                            t_.wait_ge(hsilu, uq - 1)
                        for cch in range(8):
                            ins = t_.matmul(pg[0:il, 0:512],
                                            wgT[:, par, cch, IOFF[ic]:IOFF[ic] + il],
                                            xln[:, cch, tg * 512:(tg + 1) * 512],
                                            start=(cch == 0), stop=(cch == 7))
                        ins.then_inc(hsg, 1)
                        if uq >= 2:
                            t_.wait_ge(hsh, uq - 1)
                        for cch in range(8):
                            ins = t_.matmul(pu[0:il, 0:512],
                                            wuT[:, par, cch, IOFF[ic]:IOFF[ic] + il],
                                            xln[:, cch, tg * 512:(tg + 1) * 512],
                                            start=(cch == 0), stop=(cch == 7))
                        ins.then_inc(hsu, 1)
                    EM.on("tensor", f_gu)

                    S.bump("s_silu")

                    def f_si(s, il=il, uq=uq):
                        s.wait_ge(hsg, uq)
                        if uq >= 2:
                            s.wait_ge(hsh, uq - 1)
                        s.activation(silu_g[0:il, :], pg[0:il, 0:512],
                                     AF.Silu).then_inc(hsilu, 1)
                    EM.on("scalar", f_si)

                    S.bump("s_h")

                    def f_hm(v, ic=ic, il=il, par=par, uq=uq, tg=tg, routed=routed,
                             rwb_tgt=rwb_tgt, hbuf=hbuf, rwb=rwb):
                        v.wait_ge(hsu, uq)
                        v.wait_ge(hsilu, uq)
                        if routed:
                            v.wait_ge(hrwb, rwb_tgt)
                        v.tensor_tensor(silu_g[0:il, :], silu_g[0:il, :],
                                        pu[0:il, 0:512], ALU.mult)
                        v.drain()
                        if routed:
                            v.tensor_tensor(hbuf[0:il, ic, :], silu_g[0:il, :],
                                            rwb[0:il, par, tg * 512:(tg + 1) * 512],
                                            ALU.mult).then_inc(hsh, 1)
                        else:
                            v.tensor_scalar(hbuf[0:il, ic, :], silu_g[0:il, :], 0.5,
                                            None, ALU.mult).then_inc(hsh, 1)
                    EM.on("vector", f_hm)

                h_all = S.val("s_h")
                for tt in range(4):
                    t = tg * 4 + tt
                    dq = S.val("s_d") + 1
                    S.bump("s_d")

                    def f_dn(t_, tt=tt, par=par, dq=dq, h_all=h_all, hbuf=hbuf, wdT=wdT):
                        t_.wait_ge(hsh, h_all)
                        if dq >= 3:
                            t_.wait_ge(hsx, dq - 2)
                        for half in range(2):
                            for ic in range(NI):
                                il = ILEN[ic]
                                ins = t_.matmul(pd[:, (dq % 2), half * 512:(half + 1) * 512],
                                                hbuf[0:il, ic, tt * 128:(tt + 1) * 128],
                                                wdT[0:il, par, ic, half * 512:(half + 1) * 512],
                                                start=(ic == 0), stop=(ic == NI - 1))
                        ins.then_inc(hsd, 1)
                    EM.on("tensor", f_dn)

                    S.bump("s_x")

                    def f_xe(v, t=t, dq=dq):
                        v.wait_ge(hsd, dq)
                        v.scalar_tensor_tensor(x_tm[:, t, :], pd[:, (dq % 2), :], 1.0,
                                               x_tm[:, t, :], ALU.mult, ALU.add).then_inc(hsx, 1)
                    EM.on("vector", f_xe)

    def emit_barrier(name):
        snap = [(S.h[n], S.n[n]) for n in S.n if S.n[n] > 0]

        def wait_all(e, snap=snap):
            for h, v in snap:
                e.wait_ge(h, v)
        return wait_all

    def emit_attn_a(bar_wait, xf_tile, q_fm, lat_stage, klstage, wq_sb, wkd_sb,
                    bq_row, bkd_row, pqv, ptrans):
        hac = S.get("acast")
        for _ in range(4):
            S.bump("acast", 16)
        ac_tgt = S.val("acast")

        def f_wc(g, wq_sb=wq_sb, wkd_sb=wkd_sb, bar_wait=bar_wait,
                 bq_row=bq_row, bkd_row=bkd_row):
            bar_wait(g)
            g.dma_start(out=wq_sb[:, :, :],
                        in_=wq_in.rearrange("(a p) n -> p a n", p=128)).then_inc(hac, 16)
            g.dma_start(out=wkd_sb[:, :, :],
                        in_=wkd_in.rearrange("(a p) n -> p a n", p=128)).then_inc(hac, 16)
            g.dma_start(out=bq_row[:, :], in_=bq_in[None, :]).then_inc(hac, 16)
            g.dma_start(out=bkd_row[:, :], in_=bkd_in[None, :]).then_inc(hac, 16)
            g.wait_ge(hac, ac_tgt)
        EM.on("gpsimd", f_wc)

        hv1, hs1, hv2, hap = S.get("ln_v1"), S.get("ln_s1"), S.get("ln_v2"), S.get("ln_ap")
        hlnt, hxf = S.get("lnt"), S.get("xf_ev")
        hqmm, hqev = S.get("q_mm"), S.get("q_ev")
        hlmm, hlg, hlev2 = S.get("lat_mm"), S.get("lat_gelu"), S.get("lat_ev")
        hsx = S.get("s_x")

        x_all = S.val("s_x")
        for t in range(NT):
            S.bump("ln_v1")

            def f_v1(v, t=t, first=(t == 0), x_all=x_all, bar_wait=bar_wait):
                if first:
                    bar_wait(v)
                    v.wait_ge(hsx, x_all)
                v.bn_stats(lnst[:, 0:6], x_tm[:, t, 0:512])
                v.bn_stats(lnst[:, 6:12], x_tm[:, t, 512:1024])
                v.drain()
                v.bn_aggr(lnmv[:, :], lnst[:, :])
                v.drain()
                v.tensor_scalar(lninv[:, :], lnmv[:, 1:2], 1e-5, None, ALU.add).then_inc(hv1, 1)
            EM.on("vector", f_v1)

            w1 = S.val("ln_v1"); S.bump("ln_s1")

            def f_s1(s, w1=w1, first=(t == 0), bar_wait=bar_wait):
                if first:
                    bar_wait(s)
                s.wait_ge(hv1, w1)
                s.activation(lninv[:, :], lninv[:, :], AF.Sqrt).then_inc(hs1, 1)
            EM.on("scalar", f_s1)

            w2 = S.val("ln_s1"); S.bump("ln_v2")

            def f_v2(v, w2=w2):
                v.wait_ge(hs1, w2)
                v.reciprocal(lninv[:, :], lninv[:, :])
                v.drain()
                v.tensor_scalar(lnnmi[:, :], lnmv[:, 0:1], lninv[:, 0:1], -1.0,
                                ALU.mult, ALU.mult).then_inc(hv2, 1)
            EM.on("vector", f_v2)

            w3 = S.val("ln_v2")
            wlnt_prev = S.val("lnt")
            S.bump("ln_ap")

            def f_s2(s, t=t, w3=w3, wlnt_prev=wlnt_prev):
                s.wait_ge(hv2, w3)
                s.wait_ge(hlnt, wlnt_prev)
                s.activation(xlnstage[:, :], x_tm[:, t, :], AF.Identity,
                             bias=lnnmi[:, 0:1], scale=lninv[:, 0:1]).then_inc(hap, 1)
            EM.on("scalar", f_s2)

            wap = S.val("ln_ap")
            wxf_prev = S.val("xf_ev")
            S.bump("lnt")

            def f_tr(t_, wap=wap, wxf=wxf_prev, first=(t == 0), bar_wait=bar_wait):
                if first:
                    bar_wait(t_)
                t_.wait_ge(hap, wap)
                t_.wait_ge(hxf, wxf)
                for cch in range(8):
                    ins = t_.transpose(ptrans[:, cch * 128:(cch + 1) * 128],
                                       xlnstage[:, cch * 128:(cch + 1) * 128], ident[:, :])
                ins.then_inc(hlnt, 1)
            EM.on("tensor", f_tr)

            wlnt = S.val("lnt")
            S.bump("xf_ev")

            def f_xf(v, wlnt=wlnt, xf_tile=xf_tile):
                v.wait_ge(hlnt, wlnt)
                v.tensor_copy(xf_tile[:, :, :],
                              ptrans.rearrange("p (a n) -> p a n", a=8)).then_inc(hxf, 1)
            EM.on("vector", f_xf)

            wxf2 = S.val("xf_ev")
            wlg_prev = S.val("lat_gelu")

            for co in range(8):
                wqe_prev = S.val("q_ev")
                S.bump("q_mm")

                def f_qm(t_, t=t, co=co, wxf2=wxf2, wqe=wqe_prev, wlg_prev=wlg_prev,
                         first=(t == 0 and co == 0), ac_tgt=ac_tgt,
                         xf_tile=xf_tile, wq_sb=wq_sb):
                    t_.wait_ge(hxf, wxf2)
                    t_.wait_ge(hqev, wqe)
                    if co == 0:
                        t_.wait_ge(hlg, wlg_prev)
                    if first:
                        t_.wait_ge(hac, ac_tgt)
                    ins = t_.matmul(pqv[:, 0:128], bq_row[0:1, co * 128:(co + 1) * 128],
                                    onesf[0:1, 0:128], start=True, stop=False)
                    for cch in range(8):
                        ins = t_.matmul(pqv[:, 0:128], wq_sb[:, cch, co * 128:(co + 1) * 128],
                                        xf_tile[:, cch, :], start=False, stop=(cch == 7))
                    ins.then_inc(hqmm, 1)
                EM.on("tensor", f_qm)

                wqm = S.val("q_mm")
                S.bump("q_ev")

                def f_qe(s, t=t, co=co, wqm=wqm, q_fm=q_fm):
                    s.wait_ge(hqmm, wqm)
                    s.activation(q_fm[:, co, t * 128:(t + 1) * 128], pqv[:, 0:128],
                                 AF.Identity).then_inc(hqev, 1)
                EM.on("scalar", f_qe)

            wqe_all = S.val("q_ev")
            S.bump("lat_mm")

            def f_lm(t_, t=t, wxf2=wxf2, wqe_all=wqe_all, xf_tile=xf_tile, wkd_sb=wkd_sb):
                t_.wait_ge(hxf, wxf2)
                t_.wait_ge(hqev, wqe_all)
                ins = t_.matmul(pqv[:, 0:512], onesf[0:1, 0:128], bkd_row[0:1, :],
                                start=True, stop=False)
                for cch in range(8):
                    ins = t_.matmul(pqv[:, 0:512], xf_tile[:, cch, :], wkd_sb[:, cch, :],
                                    start=False, stop=(cch == 7))
                ins.then_inc(hlmm, 1)
            EM.on("tensor", f_lm)

            wlm = S.val("lat_mm")
            wle_prev = S.val("lat_ev")
            S.bump("lat_gelu")

            def f_lg(s, wlm=wlm, wle=wle_prev, klstage=klstage):
                s.wait_ge(hlmm, wlm)
                s.wait_ge(hlev2, wle)
                s.activation(klstage[:, 0:512], pqv[:, 0:512], AF.Gelu).then_inc(hlg, 1)
            EM.on("scalar", f_lg)

            wlg = S.val("lat_gelu")
            S.bump("lat_ev")

            def f_la(v, t=t, wlg=wlg, lat_stage=lat_stage, klstage=klstage):
                v.wait_ge(hlg, wlg)
                v.tensor_tensor(lat_stage[:, t, :], klstage[:, 0:L],
                                klstage[:, L:2 * L], ALU.add).then_inc(hlev2, 1)
            EM.on("vector", f_la)

        le_all = S.val("lat_ev")
        S.bump("lat_dma", 16)
        ld_tgt = S.val("lat_dma")
        S.bump("lat_cc")

        def f_lo(g, le_all=le_all, ld_tgt=ld_tgt, lat_stage=lat_stage):
            g.wait_ge(hlev2, le_all)
            g.dma_start(lat_own.rearrange("(a p) n -> p a n", p=128),
                        lat_stage[:, :, :]).then_inc(S.h["lat_dma"], 16)
            g.wait_ge(S.h["lat_dma"], ld_tgt)
            g.collective_compute(
                "AllGather", ALU.bypass,
                replica_groups=[[0, 1], [2, 3], [4, 5], [6, 7]],
                ins=[lat_own[:, :]], outs=[lat_all[:, :]]).then_inc(S.h["lat_cc"], 1)
        EM.on("gpsimd", f_lo)

    def emit_attn_b(bar_wait, q_fm, attn_tm, lat_fm, k_sb, v_tm, pt, binm,
                    mpart, m2, negm, invl, wku_sb, wo_sb, ao_fm, vstage,
                    bku_row, bo_row, pqv2, ptr, psc, ppv, mk_pout):
        hcc, hlin, hbm = S.get("lat_cc"), S.get("lat_in"), S.get("bm_ld")
        cc_tgt = S.val("lat_cc")
        S.bump("lat_in", 16)
        S.bump("lat_in", 16)
        lin_tgt = S.val("lat_in")
        S.bump("bm_ld", 16)
        bm_tgt = S.val("bm_ld")

        def f_li(sy, cc_tgt=cc_tgt, lat_fm=lat_fm, binm=binm, bar_wait=bar_wait,
                 lin_tgt=lin_tgt, bm_tgt=bm_tgt):
            bar_wait(sy)
            sy.wait_ge(hcc, cc_tgt)
            for lch in range(2):
                sy.dma_start(lat_fm[:, lch, :], lat_all[:, lch * 128:(lch + 1) * 128],
                             transpose=True).then_inc(hlin, 16)
            sy.wait_ge(hlin, lin_tgt)
            sy.dma_start(binm[:, :, :], binm_in.rearrange("a p n -> p a n")).then_inc(hbm, 16)
            sy.wait_ge(hbm, bm_tgt)
        EM.on("sync", f_li)

        hac = S.get("acast")
        for _ in range(4):
            S.bump("acast", 16)
        ac_tgt = S.val("acast")

        def f_wc(g, wku_sb=wku_sb, wo_sb=wo_sb, bar_wait=bar_wait,
                 bku_row=bku_row, bo_row=bo_row):
            bar_wait(g)
            g.dma_start(out=wku_sb[:, :, :],
                        in_=wku_in.rearrange("(a p) n -> p a n", p=128)).then_inc(hac, 16)
            g.dma_start(out=wo_sb[:, :, :],
                        in_=wo_in.rearrange("(a p) n -> p a n", p=128)).then_inc(hac, 16)
            g.dma_start(out=bku_row[:, :], in_=bku_in[None, :]).then_inc(hac, 16)
            g.dma_start(out=bo_row[:, :], in_=bo_in[None, :]).then_inc(hac, 16)
            g.wait_ge(hac, ac_tgt)
        EM.on("gpsimd", f_wc)

        hvo = S.get("v_ones")
        S.bump("v_ones")

        def f_vo(v, v_tm=v_tm, bar_wait=bar_wait):
            bar_wait(v)
            v.memset(v_tm[:, :, 64:65], 1.0)
            v.drain()
            v.memset(v_tm[:, :, 129:130], 1.0).then_inc(hvo, 1)
        EM.on("vector", f_vo)

        hkmm, hkev = S.get("kv_mm"), S.get("kv_ev")
        hvtr, hvte = S.get("vtr"), S.get("vtr_ev")
        hscm, hp1r = S.get("scmm"), S.get("p1red")
        hm2, hgm, hnm = S.get("s_m2"), S.get("s_gmax"), S.get("s_negm")
        hexp, hmsk, hpv = S.get("s_exp"), S.get("s_msk"), S.get("s_pv")
        hinv, hoev = S.get("s_invl"), S.get("s_oev")

        psc_rel = []
        pt_rel = []
        psc_use = 0
        pt_use = 0

        for hp in range(NHP):
            oev_prev = S.val("s_oev")
            for blk, isv in ((hp, False), (8 + hp, True)):
                for tg8 in range(8):
                    wkev = S.val("kv_ev")
                    S.bump("kv_mm")

                    def f_km(t_, blk=blk, tg8=tg8,
                             first=(hp == 0 and not isv and tg8 == 0),
                             wkev=wkev, lin_tgt=lin_tgt, ac_tgt=ac_tgt,
                             lat_fm=lat_fm, wku_sb=wku_sb, bar_wait=bar_wait):
                        if first:
                            bar_wait(t_)
                            t_.wait_ge(hlin, lin_tgt)
                            t_.wait_ge(hac, ac_tgt)
                        t_.wait_ge(hkev, wkev)
                        ins = t_.matmul(pqv2[:, 0:512], bku_row[0:1, blk * 128:(blk + 1) * 128],
                                        onesf[0:1, 0:512], start=True, stop=False)
                        for lch in range(2):
                            ins = t_.matmul(pqv2[:, 0:512],
                                            wku_sb[:, lch, blk * 128:(blk + 1) * 128],
                                            lat_fm[:, lch, tg8 * 512:(tg8 + 1) * 512],
                                            start=False, stop=(lch == 1))
                        ins.then_inc(hkmm, 1)
                    EM.on("tensor", f_km)

                    wkm = S.val("kv_mm")
                    if not isv:
                        S.bump("kv_ev")

                        def f_ke(s, tg8=tg8, wkm=wkm, oev_prev=oev_prev, k_sb=k_sb):
                            s.wait_ge(hkmm, wkm)
                            s.wait_ge(hoev, oev_prev)
                            s.activation(k_sb[:, tg8 * 512:(tg8 + 1) * 512], pqv2[:, 0:512],
                                         AF.Identity).then_inc(hkev, 1)
                        EM.on("scalar", f_ke)
                    else:
                        wvte_prev = S.val("vtr_ev")
                        wvtr_prev = S.val("vtr")
                        S.bump("kv_ev")

                        def f_ve(s, wkm=wkm, wvtr=wvtr_prev, vstage=vstage):
                            s.wait_ge(hkmm, wkm)
                            s.wait_ge(hvtr, wvtr)
                            s.activation(vstage[:, 0:512], pqv2[:, 0:512],
                                         AF.Identity).then_inc(hkev, 1)
                        EM.on("scalar", f_ve)

                        wvs = S.val("kv_ev")
                        S.bump("vtr")

                        def f_vt(t_, wvs=wvs, wvte=wvte_prev, vstage=vstage):
                            t_.wait_ge(hkev, wvs)
                            t_.wait_ge(hvte, wvte)
                            for i in range(4):
                                ins = t_.transpose(ptr[:, i * 128:(i + 1) * 128],
                                                   vstage[:, i * 128:(i + 1) * 128],
                                                   ident[:, :])
                            ins.then_inc(hvtr, 1)
                        EM.on("tensor", f_vt)

                        wvt = S.val("vtr")
                        S.bump("vtr_ev")

                        def f_vv(v, tg8=tg8, wvt=wvt, oev_prev=oev_prev, v_tm=v_tm):
                            v.wait_ge(hvtr, wvt)
                            v.wait_ge(hvo, 1)
                            v.wait_ge(hoev, oev_prev)
                            v.tensor_copy(v_tm[:, tg8 * 4:(tg8 + 1) * 4, 0:64],
                                          ptr[:, 0:512].rearrange("p (a n) -> p a n", a=4)[:, :, 0:64])
                            v.drain()
                            v.tensor_copy(v_tm[:, tg8 * 4:(tg8 + 1) * 4, 65:129],
                                          ptr[:, 0:512].rearrange("p (a n) -> p a n", a=4)[:, :, 64:128]
                                          ).then_inc(hvte, 1)
                        EM.on("vector", f_vv)

            kv_done = S.val("kv_ev")
            vt_done = S.val("vtr_ev")

            for j in range(NT):
                chunks = list(range(0, j + 1)) + list(range(16, 16 + j + 1))
                nch = len(chunks)
                nq = (nch + 3) // 4
                quarters = [chunks[q * 4:(q + 1) * 4] for q in range(nq)]
                mpos = {}
                for ci, kc in enumerate(chunks):
                    if kc == j:
                        mpos.setdefault(ci // 4, []).append((ci % 4, 0))
                    elif kc == 16 + j:
                        mpos.setdefault(ci // 4, []).append((ci % 4, 1))

                for qi, qch in enumerate(quarters):
                    bank = psc_use % 2
                    rel = psc_rel[psc_use - 2] if psc_use >= 2 else None
                    psc_use += 1
                    S.bump("scmm")

                    def f_p1(t_, hp=hp, j=j, qch=qch, bank=bank, rel=rel, kvw=kv_done,
                             q_fm=q_fm, k_sb=k_sb):
                        t_.wait_ge(hkev, kvw)
                        if rel is not None:
                            t_.wait_ge(S.h[rel[0]], rel[1])
                        for ii, kc in enumerate(qch):
                            for hh in range(2):
                                ins = t_.matmul(
                                    psc[2 * hh + bank][:, ii * 128:(ii + 1) * 128],
                                    k_sb[64 * hh:64 * hh + 64, kc * 128:(kc + 1) * 128],
                                    q_fm[64 * hh:64 * hh + 64, hp, j * 128:(j + 1) * 128],
                                    start=True, stop=True,
                                    tile_position=(64 * hh, 0))
                        ins.then_inc(hscm, 1)
                    EM.on("tensor", f_p1)

                    wsc = S.val("scmm")
                    S.bump("p1red")

                    def f_r1(v, qi=qi, bank=bank, used=len(qch), wsc=wsc, mpart=mpart):
                        v.wait_ge(hscm, wsc)
                        v.tensor_reduce(mpart[:, 0, qi:qi + 1],
                                        psc[bank][:, 0:used * 128], AX, ALU.max)
                        v.drain()
                        v.tensor_reduce(mpart[:, 1, qi:qi + 1],
                                        psc[2 + bank][:, 0:used * 128], AX,
                                        ALU.max).then_inc(hp1r, 1)
                    EM.on("vector", f_r1)
                    psc_rel.append(("p1red", S.val("p1red")))

                wnm_prev = S.val("s_negm")
                S.bump("s_m2")

                def f_m2(v, nq=nq, wnm=wnm_prev, mpart=mpart, m2=m2):
                    v.wait_ge(hnm, wnm)
                    v.tensor_reduce(m2[:, 0:1], mpart[:, 0, 0:nq], AX, ALU.max)
                    v.drain()
                    v.tensor_reduce(m2[:, 1:2], mpart[:, 1, 0:nq], AX, ALU.max).then_inc(hm2, 1)
                EM.on("vector", f_m2)

                wm2 = S.val("s_m2")
                S.bump("s_gmax")

                def f_gm(g, wm2=wm2, m2=m2):
                    g.wait_ge(hm2, wm2)
                    g.partition_all_reduce(m2[:, 0:2], m2[:, 0:2], channels=128,
                                           reduce_op=ReduceOp.max).then_inc(hgm, 1)
                EM.on("gpsimd", f_gm)

                wgm = S.val("s_gmax")
                S.bump("s_negm")

                def f_nm(s, wgm=wgm, m2=m2, negm=negm):
                    s.wait_ge(hgm, wgm)
                    s.mul(negm[:, 0:2], m2[:, 0:2], -SCALE).then_inc(hnm, 1)
                    s.drain()
                EM.on("scalar", f_nm)
                wnm = S.val("s_negm")

                oev_slot = S.val("s_oev")
                exp_tgts = []
                msk_map = {}

                def emit_p2(qi):
                    nonlocal psc_use, pt_use
                    qch = quarters[qi]
                    bank = psc_use % 2
                    rel = psc_rel[psc_use - 2] if psc_use >= 2 else None
                    psc_use += 1
                    S.bump("scmm")

                    def f_p2(t_, qch=qch, bank=bank, rel=rel, hp=hp, j=j,
                             q_fm=q_fm, k_sb=k_sb):
                        if rel is not None:
                            t_.wait_ge(S.h[rel[0]], rel[1])
                        for ii, kc in enumerate(qch):
                            for hh in range(2):
                                ins = t_.matmul(
                                    psc[2 * hh + bank][:, ii * 128:(ii + 1) * 128],
                                    k_sb[64 * hh:64 * hh + 64, kc * 128:(kc + 1) * 128],
                                    q_fm[64 * hh:64 * hh + 64, hp, j * 128:(j + 1) * 128],
                                    start=True, stop=True,
                                    tile_position=(64 * hh, 0))
                        ins.then_inc(hscm, 1)
                    EM.on("tensor", f_p2)

                    wsc2 = S.val("scmm")
                    slot = pt_use % 2
                    ptrel = pt_rel[pt_use - 2] if pt_use >= 2 else None
                    pt_use += 1
                    S.bump("s_exp")

                    def f_ex(s, bank=bank, slot=slot, used=len(qch), wsc2=wsc2,
                             wnm=wnm, ptrel=ptrel, pt=pt, negm=negm):
                        s.wait_ge(hscm, wsc2)
                        s.wait_ge(hnm, wnm)
                        if ptrel is not None:
                            s.wait_ge(S.h[ptrel[0]], ptrel[1])
                        s.activation(pt[:, slot, 0, 0:used * 128], psc[bank][:, 0:used * 128],
                                     AF.Exp, bias=negm[:, 0:1], scale=SCALE)
                        s.drain()
                        s.activation(pt[:, slot, 1, 0:used * 128],
                                     psc[2 + bank][:, 0:used * 128],
                                     AF.Exp, bias=negm[:, 1:2], scale=SCALE).then_inc(hexp, 1)
                    EM.on("scalar", f_ex)
                    psc_rel.append(("s_exp", S.val("s_exp")))
                    exp_tgts.append(S.val("s_exp"))

                    if qi in mpos:
                        wexp = S.val("s_exp")
                        S.bump("s_msk")

                        def f_mk(v, slot=slot, wexp=wexp, ents=tuple(mpos[qi]),
                                 bm_tgt=bm_tgt, pt=pt, binm=binm):
                            v.wait_ge(hexp, wexp)
                            v.wait_ge(hbm, bm_tgt)
                            ops = [(ii, which, hh) for (ii, which) in ents for hh in range(2)]
                            for n_, (ii, which, hh) in enumerate(ops):
                                ins = v.tensor_tensor(
                                    pt[:, slot, hh, ii * 128:(ii + 1) * 128],
                                    pt[:, slot, hh, ii * 128:(ii + 1) * 128],
                                    binm[:, which, :], ALU.mult)
                                if n_ == len(ops) - 1:
                                    ins.then_inc(hmsk, 1)
                                else:
                                    v.drain()
                        EM.on("vector", f_mk)
                        msk_map[qi] = S.val("s_msk")

                def emit_pv(qi, slot_of):
                    qch = quarters[qi]
                    slot = slot_of
                    wexp = exp_tgts[qi]
                    wmsk = msk_map.get(qi)
                    S.bump("s_pv")

                    def f_pv(t_, qch=qch, slot=slot, wexp=wexp, wmsk=wmsk,
                             isfirst=(qi == 0), islast=(qi == nq - 1),
                             oev_slot=oev_slot, vtw=vt_done, pt=pt, v_tm=v_tm):
                        t_.wait_ge(hexp, wexp)
                        if wmsk is not None:
                            t_.wait_ge(hmsk, wmsk)
                        if isfirst:
                            t_.wait_ge(hoev, oev_slot)
                            t_.wait_ge(hvte, vtw)
                        for hh in range(2):
                            for ii in range(len(qch)):
                                kc = qch[ii]
                                ins = t_.matmul(ppv[hh][:, 0:65],
                                                pt[:, slot, hh, ii * 128:(ii + 1) * 128],
                                                v_tm[:, kc, 65 * hh:65 * hh + 65],
                                                start=(isfirst and ii == 0),
                                                stop=(islast and ii == len(qch) - 1))
                        ins.then_inc(hpv, 1)
                    EM.on("tensor", f_pv)
                    pt_rel.append(("s_pv", S.val("s_pv")))

                slots = []
                for qi in range(nq):
                    slots.append((pt_use) % 2)
                    emit_p2(qi)
                    if qi >= 1:
                        emit_pv(qi - 1, slots[qi - 1])
                emit_pv(nq - 1, slots[nq - 1])

                wpv = S.val("s_pv")
                woev_inv = S.val("s_oev")
                S.bump("s_invl")

                def f_iv(v, wpv=wpv, woev=woev_inv, invl=invl):
                    v.wait_ge(hpv, wpv)
                    v.wait_ge(hoev, woev)
                    v.reciprocal(invl[:, 0:1], ppv[0][:, 64:65])
                    v.drain()
                    v.reciprocal(invl[:, 1:2], ppv[1][:, 64:65]).then_inc(hinv, 1)
                EM.on("vector", f_iv)

                winv = S.val("s_invl")
                S.bump("s_oev", 2)

                def f_oe(s, hp=hp, j=j, winv=winv, attn_tm=attn_tm, invl=invl):
                    s.wait_ge(hinv, winv)
                    s.activation(attn_tm[:, j, 128 * hp:128 * hp + 64], ppv[0][:, 0:64],
                                 AF.Identity, scale=invl[:, 0:1]).then_inc(hoev, 1)
                    s.drain()
                    s.activation(attn_tm[:, j, 128 * hp + 64:128 * hp + 128], ppv[1][:, 0:64],
                                 AF.Identity, scale=invl[:, 1:2]).then_inc(hoev, 1)
                EM.on("scalar", f_oe)

        pout = mk_pout()
        oev_all = S.val("s_oev")
        hat, hae = S.get("ao_tr"), S.get("ao_ev")
        hom, hox = S.get("o_mm"), S.get("o_x")
        for t in range(NT):
            wae_prev = S.val("ao_ev")
            S.bump("ao_tr")

            def f_at(t_, t=t, first=(t == 0), oev_all=oev_all, wae=wae_prev,
                     attn_tm=attn_tm):
                if first:
                    t_.wait_ge(hoev, oev_all)
                t_.wait_ge(hae, wae)
                for cch in range(8):
                    ins = t_.transpose(ptr[:, cch * 128:(cch + 1) * 128],
                                       attn_tm[:, t, cch * 128:(cch + 1) * 128], ident[:, :])
                ins.then_inc(hat, 1)
            EM.on("tensor", f_at)

            wat = S.val("ao_tr")
            S.bump("ao_ev")

            def f_ae(v, wat=wat, ao_fm=ao_fm):
                v.wait_ge(hat, wat)
                v.tensor_copy(ao_fm[:, :, :],
                              ptr.rearrange("p (a n) -> p a n", a=8)).then_inc(hae, 1)
            EM.on("vector", f_ae)

            wae = S.val("ao_ev")
            oq = S.val("o_mm") + 1
            S.bump("o_mm")

            def f_om(t_, t=t, wae=wae, oq=oq, ao_fm=ao_fm, wo_sb=wo_sb):
                t_.wait_ge(hae, wae)
                if oq >= 3:
                    t_.wait_ge(hox, oq - 2)
                for half in range(2):
                    ins = t_.matmul(pout[:, oq % 2, half * 512:(half + 1) * 512],
                                    onesf[0:1, 0:128], bo_row[0:1, half * 512:(half + 1) * 512],
                                    start=True, stop=False)
                    for cch in range(8):
                        ins = t_.matmul(pout[:, oq % 2, half * 512:(half + 1) * 512],
                                        ao_fm[:, cch, :],
                                        wo_sb[:, cch, half * 512:(half + 1) * 512],
                                        start=False, stop=(cch == 7))
                ins.then_inc(hom, 1)
            EM.on("tensor", f_om)

            S.bump("o_x")

            def f_ox(v, t=t, oq=oq):
                v.wait_ge(hom, oq)
                v.scalar_tensor_tensor(x_tm[:, t, :], pout[:, oq % 2, :], 1.0,
                                       x_tm[:, t, :], ALU.mult, ALU.add).then_inc(hox, 1)
            EM.on("vector", f_ox)

    def emit_store():
        hsx, hst = S.get("s_x"), S.get("st")
        x_all = S.val("s_x")
        S.bump("st", 16)

        def f_st(sy, x_all=x_all):
            sy.wait_ge(hsx, x_all)
            sy.dma_start(y_out.rearrange("(a p) n -> p a n", p=128),
                         x_tm[:, :, :]).then_inc(hst, 16)
            sy.wait_ge(hst, 16)
        EM.on("sync", f_st)

    def mk_rt(stack, sfx=""):
        return {
            "s": sb(stack, "rt_s" + sfx, [128, NER], F32),
            "sfc": sb(stack, "rt_sfc" + sfx, [128, NER], F32)[:, :],
            "top8": sb(stack, "rt_top8" + sfx, [128, 8], F32)[:, :],
            "gs": sb(stack, "rt_gs" + sfx, [128, 8], F32)[:, :],
            "gidx": sb(stack, "rt_gidx" + sfx, [128, 8], U32)[:, :],
            "gidxf": sb(stack, "rt_gidxf" + sfx, [128, 8], F32)[:, :],
            "emask": sb(stack, "rt_emask" + sfx, [128, NER], F32)[:, :],
            "tmp": sb(stack, "rt_tmp" + sfx, [128, NER], F32)[:, :],
            "masked": sb(stack, "rt_masked" + sfx, [128, NER], F32)[:, :],
            "v8": sb(stack, "rt_v8" + sfx, [128, 8], F32)[:, :],
            "i8": sb(stack, "rt_i8" + sfx, [128, 8], U32)[:, :],
            "i8f": sb(stack, "rt_i8f" + sfx, [128, 8], F32)[:, :],
            "sel": sb(stack, "rt_sel" + sfx, [128, NER], F32)[:, :],
            "sw": sb(stack, "rt_sw" + sfx, [128, NER], F32)[:, :],
            "rs": sb(stack, "rt_rs" + sfx, [128, 1], F32)[:, :],
            "inv": sb(stack, "rt_inv" + sfx, [128, 1], F32)[:, :],
        }

    with ExitStack() as f1:
        xln = sb(f1, "xln1", [128, 8, TPC], BF16)
        gateT = sb(f1, "gateT1", [128, 8, NER], BF16)
        rt = mk_rt(f1)
        rw = sb(f1, "rw1", [128, NT, NER], F32)
        rwT = sb(f1, "rwT1", [NER, NT, 128], BF16)
        wgT = sb(f1, "wgT1", [128, 2, 8, I_E], BF16)
        wuT = sb(f1, "wuT1", [128, 2, 8, I_E], BF16)
        wdT = sb(f1, "wdT1", [128, 2, NI, C], BF16)
        hbuf = sb(f1, "hbuf1", [128, NI, 512], BF16)
        silu_g = sb(f1, "silu1", [128, 512], F32)
        rwb = sb(f1, "rwb1", [128, 2, TPC], BF16)
        rwrow = sb(f1, "rwrow1", [1, 2, TPC], BF16)
        pg = psum(f1, "pg1", [128, 512], F32)
        pu = psum(f1, "pu1", [128, 512], F32)
        pd = psum(f1, "pd1", [128, 2, C], F32)
        proute = psum(f1, "proute1", [128, 512], F32)
        ptrans = psum(f1, "ptrans1", [128, 1024], BF16)

        ld_all = emit_phase0()
        emit_gbb(proute, ld_all)
        emit_ln_route(0, xln, gateT, rt, rw, rwT, proute, ptrans,
                      (S.get("ld"), ld_all), None)
        emit_ffn(0, xln, rw, rwT, wgT, wuT, wdT, hbuf, silu_g, rwb, rwrow,
                 pg, pu, pd)
        bar1 = emit_barrier("bar1")

    with ExitStack() as at:
        q_fm = sb(at, "q_fm", [128, 8, TPC], BF16)
        attn_tm = sb(at, "attn_tm", [128, NT, C], BF16)
        lat_fm = sb(at, "lat_fm", [128, 2, T], BF16)

        with ExitStack() as ata:
            xf_tile = sb(ata, "xf_tile", [128, 8, 128], BF16)
            lat_stage = sb(ata, "lat_stage", [128, NT, L], BF16)
            klstage = sb(ata, "klstage", [128, 512], F32)
            wq_sb = sb(ata, "wq_sb", [128, 8, C], BF16)
            wkd_sb = sb(ata, "wkd_sb", [128, 8, 2 * L], BF16)
            bq_row = sb(ata, "bq_row", [1, C], F32)
            bkd_row = sb(ata, "bkd_row", [1, 2 * L], F32)
            pqv = psum(ata, "pqv", [128, 512], F32)
            ptrans2 = psum(ata, "ptrans2", [128, 1024], BF16)
            emit_attn_a(bar1, xf_tile, q_fm, lat_stage, klstage, wq_sb, wkd_sb,
                        bq_row, bkd_row, pqv, ptrans2)
            bar2 = emit_barrier("bar2")

        with ExitStack() as atb:
            k_sb = sb(atb, "k_sb", [128, T], BF16)
            v_tm = sb(atb, "v_tm", [128, 32, 130], BF16)
            pt = sb(atb, "pt", [128, 2, 2, 512], BF16)
            binm = sb(atb, "binm_sb", [128, 2, 128], BF16)
            mpart = sb(atb, "mpart", [128, 2, 9], F32)
            m2 = sb(atb, "m2", [128, 2], F32)
            negm = sb(atb, "negm", [128, 2], F32)
            invl = sb(atb, "invl", [128, 2], F32)
            wku_sb = sb(atb, "wku_sb", [128, 2, 2 * C], BF16)
            wo_sb = sb(atb, "wo_sb", [128, 8, C], BF16)
            bku_row = sb(atb, "bku_row", [1, 2 * C], F32)
            bo_row = sb(atb, "bo_row", [1, C], F32)
            ao_fm = xlnstage.rearrange("p (a n) -> p a n", a=8)
            vstage = xlnstage[:, 0:512]
            pqv2 = psum(atb, "pqv2", [128, 512], F32)
            ptr = psum(atb, "ptr", [128, 1024], BF16)
            sc_stack = ExitStack()
            psc = [psum(sc_stack, f"psc{i}", [128, 512], F32) for i in range(4)]
            ppv = [psum(sc_stack, f"ppv{i}", [128, 65], F32) for i in range(2)]
            pout_holder = []

            def mk_pout():
                sc_stack.close()
                p = psum(atb, "pout", [128, 2, C], F32)
                pout_holder.append(p)
                return p

            emit_attn_b(bar2, q_fm, attn_tm, lat_fm, k_sb, v_tm, pt, binm,
                        mpart, m2, negm, invl, wku_sb, wo_sb, ao_fm, vstage,
                        bku_row, bo_row, pqv2, ptr, psc, ppv, mk_pout)
            bar3 = emit_barrier("bar3")

    with ExitStack() as f3:
        xln3 = sb(f3, "xln3", [128, 8, TPC], BF16)
        gateT3 = sb(f3, "gateT3", [128, 8, NER], BF16)
        rt3 = mk_rt(f3, "3")
        rw3 = sb(f3, "rw3", [128, NT, NER], F32)
        rwT3 = sb(f3, "rwT3", [NER, NT, 128], BF16)
        wgT3 = sb(f3, "wgT3", [128, 2, 8, I_E], BF16)
        wuT3 = sb(f3, "wuT3", [128, 2, 8, I_E], BF16)
        wdT3 = sb(f3, "wdT3", [128, 2, NI, C], BF16)
        hbuf3 = sb(f3, "hbuf3", [128, NI, 512], BF16)
        silu3 = sb(f3, "silu3", [128, 512], F32)
        rwb3 = sb(f3, "rwb3", [128, 2, TPC], BF16)
        rwrow3 = sb(f3, "rwrow3", [1, 2, TPC], BF16)
        pg3 = psum(f3, "pg3", [128, 512], F32)
        pu3 = psum(f3, "pu3", [128, 512], F32)
        pd3 = psum(f3, "pd3", [128, 2, C], F32)
        proute3 = psum(f3, "proute3", [128, 512], F32)
        ptrans3 = psum(f3, "ptrans3", [128, 1024], BF16)

        emit_ln_route(1, xln3, gateT3, rt3, rw3, rwT3, proute3, ptrans3,
                      (S.get("o_x"), S.val("o_x")), bar3)
        emit_ffn(1, xln3, rw3, rwT3, wgT3, wuT3, wdT3, hbuf3, silu3, rwb3,
                 rwrow3, pg3, pu3, pd3)

    emit_store()

    with nc.Block() as block:
        EM.emit(block)

    cx.close()
    nc.finalize()
    mybir.codegen_inst_isa_subclasses(nc)
    return nc


_NC_CACHE = []


def _get_nc():
    if not _NC_CACHE:
        _NC_CACHE.append(build_program())
    return _NC_CACHE[0]


def kernel(**inputs):
    x = np.ascontiguousarray(np.asarray(inputs["L0_attn_x"], dtype=np.float32))
    B_, T_, C_ = x.shape
    assert (B_, T_, C_) == (4, T, C)

    common = {
        "wq": np.ascontiguousarray(inputs["L0_attn_Wq"], dtype=np.float32),
        "bq": np.ascontiguousarray(inputs["L0_attn_bq"], dtype=np.float32),
        "wkd": np.ascontiguousarray(inputs["L0_attn_Wkd"], dtype=np.float32),
        "bkd": np.ascontiguousarray(inputs["L0_attn_bkd"], dtype=np.float32),
        "wku": np.ascontiguousarray(inputs["L0_attn_Wku"], dtype=np.float32),
        "bku": np.ascontiguousarray(inputs["L0_attn_bku"], dtype=np.float32),
        "wo": np.ascontiguousarray(inputs["L0_attn_Wo"], dtype=np.float32),
        "bo": np.ascontiguousarray(inputs["L0_attn_bo"], dtype=np.float32),
        "gw": np.ascontiguousarray(inputs["L0_ffn_gate_w"], dtype=np.float32),
        "gb": np.ascontiguousarray(inputs["L0_ffn_gate_bias"], dtype=np.float32),
        "wg": np.ascontiguousarray(inputs["L0_ffn_Wg"], dtype=np.float32),
        "wu": np.ascontiguousarray(inputs["L0_ffn_Wu"], dtype=np.float32),
        "wd": np.ascontiguousarray(inputs["L0_ffn_Wd"], dtype=np.float32),
        "sg": np.ascontiguousarray(inputs["L0_ffn_Sg"], dtype=np.float32),
        "su": np.ascontiguousarray(inputs["L0_ffn_Su"], dtype=np.float32),
        "sd": np.ascontiguousarray(inputs["L0_ffn_Sd"], dtype=np.float32),
        "ident": np.eye(128, dtype=ml_dtypes.bfloat16),
        "iota64": np.tile(np.arange(NER, dtype=np.float32), (128, 1)),
        "iotag": np.tile((np.arange(NER) // 8).astype(np.float32), (128, 1)),
        "onesf": np.ones((1, 512), dtype=np.float32),
        "identf": np.eye(128, dtype=np.float32),
    }

    tri = np.triu(np.ones((128, 128), dtype=np.float32)).astype(ml_dtypes.bfloat16)
    ones_t = np.ones((128, 128), dtype=ml_dtypes.bfloat16)
    zeros_t = np.zeros((128, 128), dtype=ml_dtypes.bfloat16)
    binm_even = np.stack([tri, zeros_t])
    binm_odd = np.stack([ones_t, tri])

    xt = x.reshape(4, 32, 128, C)
    in_maps = []
    for core in range(8):
        b, p = core // 2, core % 2
        xs = np.ascontiguousarray(xt[b, p::2].reshape(TPC, C))
        m = dict(common)
        m["x"] = xs
        m["binm"] = np.ascontiguousarray(binm_even if p == 0 else binm_odd)
        in_maps.append(m)

    nc = _get_nc()
    res = run_bass_kernel_spmd(nc, in_maps, core_ids=list(range(8)))

    out = np.empty((4, 32, 128, C), dtype=np.float32)
    for core in range(8):
        b, p = core // 2, core % 2
        out[b, p::2] = res.results[core]["y"].reshape(NT, 128, C)
    return out.reshape(4, T, C)


if __name__ == "__main__":
    import time
    t0 = time.time()
    nc = build_program()
    print("built in", time.time() - t0, "s")
